# version 01 (final)
# baseline (speedup 1.0000x reference)
import numpy as np
import ml_dtypes
from contextlib import ExitStack
import concourse.bass as bass
import concourse.mybir as mybir
from concourse.bass_utils import run_bass_kernel_spmd
from concourse.bass import IndirectOffsetOnAxis

F32 = mybir.dt.float32
BF16 = mybir.dt.bfloat16
U32 = mybir.dt.uint32
AL = mybir.AluOpType
AF = mybir.ActivationFunctionType
BF = ml_dtypes.bfloat16

B, T, C = 4, 8192, 1024
TT = B * T
NTOK = TT // 8
E, TOPK, CAP = 64, 8, 512
ISH = 2048
IEX = 512

AXX = mybir.AxisListType.X


class P:

    def __init__(self, nc):
        self.nc = nc
        self.ops = []

    def v(self, fn):
        self.ops.append(("vector", fn, 1))

    def s(self, fn):
        self.ops.append(("scalar", fn, 1))

    def t(self, fn):
        self.ops.append(("tensor", fn, 1))

    def g(self, fn):
        self.ops.append(("gpsimd", fn, 1))

    def dma(self, fn):
        self.ops.append(("sync", fn, 16))

    def gdma(self, fn):
        self.ops.append(("gpsimd", fn, 16))

    def build(self):
        nc = self.nc
        total = sum(o[2] for o in self.ops)
        with nc.semaphore("S") as sem, nc.Block() as block:
            def mk(name):
                def body(eng):
                    c = 0
                    for en, fn, inc in self.ops:
                        if en == name:
                            eng.wait_ge(sem, c)
                            r = fn(eng)
                            r.then_inc(sem, inc)
                        c += inc
                    if name == "vector":
                        eng.wait_ge(sem, total)
                return body
            block.vector(mk("vector"))
            block.scalar(mk("scalar"))
            block.tensor(mk("tensor"))
            block.gpsimd(mk("gpsimd"))
            block.sync(mk("sync"))


def _run(nc, in_maps, outs):
    res = run_bass_kernel_spmd(nc, in_maps, core_ids=list(range(8)))
    return [[np.asarray(res.results[i][o]) for i in range(8)] for o in outs]


def build_A():
    nc = bass.Bass("TRN2", target_bir_lowering=False)
    xin = nc.dram_tensor("xin", [NTOK, C], F32, kind="ExternalInput").ap()
    xn = nc.dram_tensor("xn", [NTOK, C], F32, kind="ExternalOutput").ap()
    p = P(nc)
    with ExitStack() as es:
        xt = es.enter_context(nc.sbuf_tensor("xt", [128, C], F32))
        xc = es.enter_context(nc.sbuf_tensor("xc", [128, C], F32))
        sq = es.enter_context(nc.sbuf_tensor("sq", [128, C], F32))
        st = es.enter_context(nc.sbuf_tensor("st", [128, 8], F32))
        for i in range(NTOK // 128):
            r0 = i * 128
            p.dma(lambda e, r0=r0: e.dma_start(out=xt[:, :], in_=xin[r0:r0 + 128, :]))
            p.v(lambda e: e.reduce_sum(st[:, 0:1], xt[:, :], axis=AXX))
            p.v(lambda e: e.tensor_scalar_mul(st[:, 1:2], st[:, 0:1], 1.0 / C))
            p.v(lambda e: e.tensor_scalar(xc[:, :], xt[:, :], st[:, 1:2], None, op0=AL.subtract))
            p.s(lambda e: e.activation(sq[:, :], xc[:, :], AF.Square))
            p.v(lambda e: e.reduce_sum(st[:, 2:3], sq[:, :], axis=AXX))
            p.v(lambda e: e.tensor_scalar(st[:, 3:4], st[:, 2:3], 1.0 / C, 1e-5, op0=AL.mult, op1=AL.add))
            p.s(lambda e: e.activation(st[:, 4:5], st[:, 3:4], AF.Sqrt))
            p.v(lambda e: e.reciprocal(st[:, 5:6], st[:, 4:5]))
            p.v(lambda e: e.tensor_scalar_mul(sq[:, :], xc[:, :], st[:, 5:6]))
            p.dma(lambda e, r0=r0: e.dma_start(out=xn[r0:r0 + 128, :], in_=sq[:, :]))
        p.build()
    return nc


def build_B():
    nc = bass.Bass("TRN2", target_bir_lowering=False)
    xnT = nc.dram_tensor("xnT", [C, NTOK], F32, kind="ExternalInput").ap()
    gw = nc.dram_tensor("gw", [C, E], F32, kind="ExternalInput").ap()
    io64 = nc.dram_tensor("io64", [128, E], F32, kind="ExternalInput").ap()
    sel = nc.dram_tensor("sel", [NTOK, E], F32, kind="ExternalOutput").ap()
    eidx = nc.dram_tensor("eidx", [NTOK, TOPK], F32, kind="ExternalOutput").ap()
    wraw = nc.dram_tensor("wraw", [NTOK, TOPK], F32, kind="ExternalOutput").ap()
    p = P(nc)
    with ExitStack() as es:
        xs = es.enter_context(nc.sbuf_tensor("xs", [128, 8 * NTOK // 8], F32))
        gws = es.enter_context(nc.sbuf_tensor("gws", [128, 8 * E], F32))
        iot = es.enter_context(nc.sbuf_tensor("iot", [128, E], F32))
        pr = es.enter_context(nc.sbuf_tensor("pr", [128, E], F32))
        wk = es.enter_context(nc.sbuf_tensor("wk", [128, E], F32))
        eq = es.enter_context(nc.sbuf_tensor("eq", [128, E], F32))
        tmp = es.enter_context(nc.sbuf_tensor("tmp", [128, E], F32))
        cols = es.enter_context(nc.sbuf_tensor("cols", [128, 8], F32))
        eixt = es.enter_context(nc.sbuf_tensor("eixt", [128, TOPK], F32))
        wrt = es.enter_context(nc.sbuf_tensor("wrt", [128, TOPK], F32))
        ps = es.enter_context(nc.psum_tensor("ps", [128, E], F32))
        for g in range(8):
            p.dma(lambda e, g=g: e.dma_start(out=gws[:, g * E:(g + 1) * E], in_=gw[g * 128:(g + 1) * 128, :]))
        p.dma(lambda e: e.dma_start(out=iot[:, :], in_=io64[:, :]))
        xgs = []
        for g in range(8):
            xgs.append(es.enter_context(nc.sbuf_tensor("xg%d" % g, [128, NTOK], F32)))
        for g in range(8):
            p.dma(lambda e, g=g: e.dma_start(out=xgs[g][:, :], in_=xnT[g * 128:(g + 1) * 128, :]))
        for i in range(NTOK // 128):
            t0 = i * 128
            for g in range(8):
                p.t(lambda e, g=g, t0=t0: e.matmul(ps[:, :], xgs[g][:, t0:t0 + 128],
                                                   gws[:, g * E:(g + 1) * E],
                                                   start=(g == 0), stop=(g == 7)))
            p.v(lambda e: e.tensor_copy(pr[:, :], ps[:, :]))
            p.v(lambda e: e.reduce_max(cols[:, 0:1], pr[:, :], axis=AXX))
            p.v(lambda e: e.tensor_scalar(pr[:, :], pr[:, :], cols[:, 0:1], None, op0=AL.subtract))
            p.s(lambda e: e.activation(pr[:, :], pr[:, :], AF.Exp))
            p.v(lambda e: e.reduce_sum(cols[:, 1:2], pr[:, :], axis=AXX))
            p.v(lambda e: e.reciprocal(cols[:, 2:3], cols[:, 1:2]))
            p.v(lambda e: e.tensor_scalar_mul(pr[:, :], pr[:, :], cols[:, 2:3]))
            p.v(lambda e: e.tensor_copy(wk[:, :], pr[:, :]))
            for r in range(TOPK):
                p.v(lambda e, r=r: e.reduce_max(wrt[:, r:r + 1], wk[:, :], axis=AXX))
                p.v(lambda e, r=r: e.tensor_scalar(eq[:, :], wk[:, :], wrt[:, r:r + 1], None, op0=AL.is_ge))
                p.v(lambda e: e.tensor_tensor(tmp[:, :], eq[:, :], iot[:, :], op=AL.mult))
                p.v(lambda e, r=r: e.reduce_sum(eixt[:, r:r + 1], tmp[:, :], axis=AXX))
                p.v(lambda e: e.tensor_scalar_mul(eq[:, :], eq[:, :], 2.0))
                p.v(lambda e: e.tensor_tensor(wk[:, :], wk[:, :], eq[:, :], op=AL.subtract))
            p.v(lambda e: e.tensor_scalar(eq[:, :], pr[:, :], wrt[:, TOPK - 1:TOPK], None, op0=AL.is_ge))
            p.dma(lambda e, t0=t0: e.dma_start(out=sel[t0:t0 + 128, :], in_=eq[:, :]))
            p.dma(lambda e, t0=t0: e.dma_start(out=eidx[t0:t0 + 128, :], in_=eixt[:, :]))
            p.dma(lambda e, t0=t0: e.dma_start(out=wraw[t0:t0 + 128, :], in_=wrt[:, :]))
        p.build()
        for h in xgs:
            pass
    return nc



def build_C2():
    nc = bass.Bass("TRN2", target_bir_lowering=False)
    sall = nc.dram_tensor("sall", [TT, E], F32, kind="ExternalInput").ap()
    sown = nc.dram_tensor("sown", [NTOK, E], F32, kind="ExternalInput").ap()
    eidx = nc.dram_tensor("eidx", [NTOK, TOPK], F32, kind="ExternalInput").ap()
    wraw = nc.dram_tensor("wraw", [NTOK, TOPK], F32, kind="ExternalInput").ap()
    ustr = nc.dram_tensor("ustr", [128, 128], F32, kind="ExternalInput").ap()
    onec = nc.dram_tensor("onec", [128, 1], F32, kind="ExternalInput").ap()
    one1 = nc.dram_tensor("one1", [1, 128], F32, kind="ExternalInput").ap()
    pmask = nc.dram_tensor("pmask", [256, 32], F32, kind="ExternalInput").ap()
    io64 = nc.dram_tensor("io64", [128, E], F32, kind="ExternalInput").ap()
    tok0 = nc.dram_tensor("tok0", [128, 1], F32, kind="ExternalInput").ap()
    id128 = nc.dram_tensor("id128", [128, 128], F32, kind="ExternalInput").ap()
    inv = nc.dram_tensor("inv", [E, CAP + 1], F32, kind="ExternalOutput").ap()
    w8 = nc.dram_tensor("w8", [NTOK, TOPK], F32, kind="ExternalOutput").ap()
    slo = nc.dram_tensor("slo", [NTOK, TOPK], F32, kind="ExternalOutput").ap()
    p = P(nc)
    with ExitStack() as es:
        stile = es.enter_context(nc.sbuf_tensor("stile", [128, E], F32))
        onecs = es.enter_context(nc.sbuf_tensor("onecs", [128, 1], F32))
        one1s = es.enter_context(nc.sbuf_tensor("one1s", [1, 128], F32))
        us = es.enter_context(nc.sbuf_tensor("us", [128, 128], F32))
        idn = es.enter_context(nc.sbuf_tensor("idn", [128, 128], F32))
        pmA = es.enter_context(nc.sbuf_tensor("pmA", [128, 32], F32))
        pmB = es.enter_context(nc.sbuf_tensor("pmB", [128, 32], F32))
        cnt = es.enter_context(nc.sbuf_tensor("cnt", [64, 256], F32))
        ctA = es.enter_context(nc.sbuf_tensor("ctA", [128, 64], F32))
        ctB = es.enter_context(nc.sbuf_tensor("ctB", [128, 64], F32))
        brow = es.enter_context(nc.sbuf_tensor("brow", [1, 64], F32))
        pos = es.enter_context(nc.sbuf_tensor("pos", [128, E], F32))
        iot = es.enter_context(nc.sbuf_tensor("iot", [128, E], F32))
        eixt = es.enter_context(nc.sbuf_tensor("eixt", [128, TOPK], F32))
        wrt = es.enter_context(nc.sbuf_tensor("wrt", [128, TOPK], F32))
        eq = es.enter_context(nc.sbuf_tensor("eq", [128, E], F32))
        tmp = es.enter_context(nc.sbuf_tensor("tmp", [128, E], F32))
        cols = es.enter_context(nc.sbuf_tensor("cols", [128, 8], F32))
        keep8 = es.enter_context(nc.sbuf_tensor("keep8", [128, TOPK], F32))
        slc = es.enter_context(nc.sbuf_tensor("slc", [128, TOPK], F32))
        sidx = es.enter_context(nc.sbuf_tensor("sidx", [128, 1], F32))
        sidxu = es.enter_context(nc.sbuf_tensor("sidxu", [128, 1], U32))
        tokp = es.enter_context(nc.sbuf_tensor("tokp", [128, 1], F32))
        ztile = es.enter_context(nc.sbuf_tensor("ztile", [64, CAP + 1], F32))
        psc = es.enter_context(nc.psum_tensor("psc", [64, 256], F32))
        psb = es.enter_context(nc.psum_tensor("psb", [1, 64], F32))
        psp = es.enter_context(nc.psum_tensor("psp", [128, E], F32))
        pst = es.enter_context(nc.psum_tensor("pst", [128, 64], F32))
        p.dma(lambda e: e.dma_start(out=onecs[:, :], in_=onec[:, :]))
        p.dma(lambda e: e.dma_start(out=one1s[:, :], in_=one1[:, :]))
        p.dma(lambda e: e.dma_start(out=us[:, :], in_=ustr[:, :]))
        p.dma(lambda e: e.dma_start(out=idn[:, :], in_=id128[:, :]))
        p.dma(lambda e: e.dma_start(out=pmA[:, :], in_=pmask[0:128, :]))
        p.dma(lambda e: e.dma_start(out=pmB[:, :], in_=pmask[128:256, :]))
        p.dma(lambda e: e.dma_start(out=iot[:, :], in_=io64[:, :]))
        p.dma(lambda e: e.dma_start(out=tokp[:, :], in_=tok0[:, :]))
        p.v(lambda e: e.memset(ztile[:, :], 0.0))
        p.dma(lambda e: e.dma_start(out=inv[:, :], in_=ztile[:, :]))
        for k in range(TT // 128):
            p.dma(lambda e, k=k: e.dma_start(out=stile[:, :], in_=sall[k * 128:(k + 1) * 128, :]))
            p.t(lambda e, k=k: e.matmul(psc[0:64, k:k + 1], stile[:, :], onecs[:, :], start=True, stop=True))
        p.v(lambda e: e.tensor_copy(cnt[:, :], psc[0:64, :]))
        p.t(lambda e: e.transpose(pst[:, 0:64], cnt[0:64, 0:128], idn[0:64, 0:64]))
        p.v(lambda e: e.tensor_copy(ctA[:, :], pst[:, 0:64]))
        p.t(lambda e: e.transpose(pst[:, 0:64], cnt[0:64, 128:256], idn[0:64, 0:64]))
        p.v(lambda e: e.tensor_copy(ctB[:, :], pst[:, 0:64]))
        for j in range(NTOK // 128):
            t0 = j * 128
            p.t(lambda e, j=j: e.matmul(psb[:, :], pmA[:, j:j + 1], ctA[:, :], start=True, stop=False))
            p.t(lambda e, j=j: e.matmul(psb[:, :], pmB[:, j:j + 1], ctB[:, :], start=False, stop=True))
            p.v(lambda e: e.tensor_copy(brow[:, :], psb[:, :]))
            p.dma(lambda e, t0=t0: e.dma_start(out=stile[:, :], in_=sown[t0:t0 + 128, :]))
            p.t(lambda e: e.matmul(psp[:, :], us[:, :], stile[:, :], start=True, stop=False))
            p.t(lambda e: e.matmul(psp[:, :], one1s[:, :], brow[:, :], start=False, stop=True))
            p.v(lambda e: e.tensor_copy(pos[:, :], psp[:, :]))
            p.dma(lambda e, t0=t0: e.dma_start(out=eixt[:, :], in_=eidx[t0:t0 + 128, :]))
            p.dma(lambda e, t0=t0: e.dma_start(out=wrt[:, :], in_=wraw[t0:t0 + 128, :]))
            p.v(lambda e, j=j: e.tensor_scalar(tmp[:, 0:1], tokp[:, :], float(1 + 128 * j), None, op0=AL.add))
            for r in range(TOPK):
                p.v(lambda e, r=r: e.tensor_scalar(eq[:, :], iot[:, :], eixt[:, r:r + 1], None, op0=AL.is_equal))
                p.v(lambda e: e.tensor_tensor(eq[:, :], eq[:, :], pos[:, :], op=AL.mult))
                p.v(lambda e, r=r: e.reduce_sum(cols[:, 0:1], eq[:, :], axis=AXX))
                p.v(lambda e, r=r: e.tensor_scalar(keep8[:, r:r + 1], cols[:, 0:1], 511.5, None, op0=AL.is_lt))
                p.v(lambda e, r=r: e.tensor_scalar(slc[:, r:r + 1], cols[:, 0:1], 512.0, None, op0=AL.min))
                p.v(lambda e, r=r: e.tensor_scalar(sidx[:, :], eixt[:, r:r + 1], float(CAP + 1), slc[:, r:r + 1], op0=AL.mult, op1=AL.add))
                p.v(lambda e: e.tensor_copy(sidxu[:, :], sidx[:, :]))
                p.gdma(lambda e: e.indirect_dma_start(inv[:, :], IndirectOffsetOnAxis(ap=sidxu[:, :], axis=1), tmp[:, 0:1], None))
            p.v(lambda e: e.reduce_sum(cols[:, 1:2], wrt[:, :], axis=AXX))
            p.v(lambda e: e.reciprocal(cols[:, 2:3], cols[:, 1:2]))
            p.v(lambda e: e.tensor_scalar_mul(wrt[:, :], wrt[:, :], cols[:, 2:3]))
            p.v(lambda e: e.tensor_tensor(wrt[:, :], wrt[:, :], keep8[:, :], op=AL.mult))
            p.dma(lambda e, t0=t0: e.dma_start(out=w8[t0:t0 + 128, :], in_=wrt[:, :]))
            p.dma(lambda e, t0=t0: e.dma_start(out=slo[t0:t0 + 128, :], in_=slc[:, :]))
        p.build()
    return nc


def build_D1():
    nc = bass.Bass("TRN2", target_bir_lowering=False)
    xov = nc.dram_tensor("xov", [C, NTOK + 1], F32, kind="ExternalInput").ap()
    wkt = nc.dram_tensor("wkt", [C, C], F32, kind="ExternalInput").ap()
    wvt = nc.dram_tensor("wvt", [C, C], F32, kind="ExternalInput").ap()
    wrt_ = nc.dram_tensor("wrt", [C, C], F32, kind="ExternalInput").ap()
    tmk = nc.dram_tensor("tmk", [C, 1], F32, kind="ExternalInput").ap()
    tmv = nc.dram_tensor("tmv", [C, 1], F32, kind="ExternalInput").ap()
    tmr = nc.dram_tensor("tmr", [C, 1], F32, kind="ExternalInput").ap()
    kout = nc.dram_tensor("kout", [C, NTOK], BF16, kind="ExternalOutput").ap()
    vout = nc.dram_tensor("vout", [C, NTOK], BF16, kind="ExternalOutput").ap()
    srout = nc.dram_tensor("srout", [C, NTOK], BF16, kind="ExternalOutput").ap()
    p = P(nc)
    xg, wkb, wvb, wrb = [], [], [], []
    ctx = []
    for g in range(8):
        h = nc.sbuf_tensor("xg%d" % g, [128, NTOK + 1], BF16); ctx.append(h); xg.append(h.__enter__())
        h = nc.sbuf_tensor("wk%d" % g, [128, C], BF16); ctx.append(h); wkb.append(h.__enter__())
        h = nc.sbuf_tensor("wv%d" % g, [128, C], BF16); ctx.append(h); wvb.append(h.__enter__())
        h = nc.sbuf_tensor("wr%d" % g, [128, C], BF16); ctx.append(h); wrb.append(h.__enter__())
    with ExitStack() as es:
        wf = es.enter_context(nc.sbuf_tensor("wf", [128, C], F32))
        wfx = es.enter_context(nc.sbuf_tensor("wfx", [128, NTOK + 1], F32))
        mixc = es.enter_context(nc.sbuf_tensor("mixc", [128, 8 * 3 * 2], F32))
        xkb = es.enter_context(nc.sbuf_tensor("xkb", [128, 8 * 512], BF16))
        xvb = es.enter_context(nc.sbuf_tensor("xvb", [128, 8 * 512], BF16))
        xrb = es.enter_context(nc.sbuf_tensor("xrb", [128, 8 * 512], BF16))
        m1 = es.enter_context(nc.sbuf_tensor("m1", [128, 512], BF16))
        m2 = es.enter_context(nc.sbuf_tensor("m2", [128, 512], BF16))
        ob = es.enter_context(nc.sbuf_tensor("ob", [128, 512], BF16))
        pk = es.enter_context(nc.psum_tensor("pk", [128, 512], F32))
        pv = es.enter_context(nc.psum_tensor("pv", [128, 512], F32))
        pr = es.enter_context(nc.psum_tensor("pr", [128, 512], F32))
        for g in range(8):
            p.dma(lambda e, g=g: e.dma_start(out=wfx[:, :], in_=xov[g * 128:(g + 1) * 128, :]))
            p.v(lambda e, g=g: e.tensor_copy(xg[g][:, :], wfx[:, :]))
            for wsb, wsrc in ((wkb, wkt), (wvb, wvt), (wrb, wrt_)):
                p.dma(lambda e, g=g, wsrc=wsrc: e.dma_start(out=wf[:, :], in_=wsrc[g * 128:(g + 1) * 128, :]))
                p.v(lambda e, g=g, wsb=wsb: e.tensor_copy(wsb[g][:, :], wf[:, :]))
            for q, tsrc in enumerate((tmk, tmv, tmr)):
                p.dma(lambda e, g=g, q=q, tsrc=tsrc: e.dma_start(out=mixc[:, g * 6 + q * 2:g * 6 + q * 2 + 1], in_=tsrc[g * 128:(g + 1) * 128, :]))
                p.v(lambda e, g=g, q=q: e.tensor_scalar(mixc[:, g * 6 + q * 2 + 1:g * 6 + q * 2 + 2], mixc[:, g * 6 + q * 2:g * 6 + q * 2 + 1], -1.0, 1.0, op0=AL.mult, op1=AL.add))
        NTC = NTOK // 512
        for tc in range(NTC):
            c0 = tc * 512
            for g in range(8):
                for q, dst in enumerate((xkb, xvb, xrb)):
                    p.v(lambda e, g=g, q=q, c0=c0: e.tensor_scalar_mul(m1[:, :], xg[g][:, c0 + 1:c0 + 513], mixc[:, g * 6 + q * 2:g * 6 + q * 2 + 1]))
                    p.v(lambda e, g=g, q=q, c0=c0: e.tensor_scalar_mul(m2[:, :], xg[g][:, c0:c0 + 512], mixc[:, g * 6 + q * 2 + 1:g * 6 + q * 2 + 2]))
                    p.v(lambda e, g=g, dst=dst: e.tensor_tensor(dst[:, g * 512:(g + 1) * 512], m1[:, :], m2[:, :], op=AL.add))
            for dg in range(8):
                for g in range(8):
                    p.t(lambda e, g=g, dg=dg: e.matmul(pk[:, :], wkb[g][:, dg * 128:(dg + 1) * 128], xkb[:, g * 512:(g + 1) * 512], start=(g == 0), stop=(g == 7)))
                p.v(lambda e: e.tensor_copy(ob[:, :], pk[:, :]))
                p.dma(lambda e, dg=dg, c0=c0: e.dma_start(out=kout[dg * 128:(dg + 1) * 128, c0:c0 + 512], in_=ob[:, :]))
                for g in range(8):
                    p.t(lambda e, g=g, dg=dg: e.matmul(pv[:, :], wvb[g][:, dg * 128:(dg + 1) * 128], xvb[:, g * 512:(g + 1) * 512], start=(g == 0), stop=(g == 7)))
                p.v(lambda e: e.tensor_copy(ob[:, :], pv[:, :]))
                p.dma(lambda e, dg=dg, c0=c0: e.dma_start(out=vout[dg * 128:(dg + 1) * 128, c0:c0 + 512], in_=ob[:, :]))
                for g in range(8):
                    p.t(lambda e, g=g, dg=dg: e.matmul(pr[:, :], wrb[g][:, dg * 128:(dg + 1) * 128], xrb[:, g * 512:(g + 1) * 512], start=(g == 0), stop=(g == 7)))
                p.s(lambda e: e.activation(ob[:, :], pr[:, :], AF.Sigmoid))
                p.dma(lambda e, dg=dg, c0=c0: e.dma_start(out=srout[dg * 128:(dg + 1) * 128, c0:c0 + 512], in_=ob[:, :]))
        p.build()
    for h in reversed(ctx):
        h.__exit__(None, None, None)
    return nc


def build_D2():
    nc = bass.Bass("TRN2", target_bir_lowering=False)
    kT = nc.dram_tensor("kT", [512, T], BF16, kind="ExternalInput").ap()
    vT = nc.dram_tensor("vT", [512, T], BF16, kind="ExternalInput").ap()
    sT = nc.dram_tensor("sT", [512, T], BF16, kind="ExternalInput").ap()
    td = nc.dram_tensor("td", [512, 1], F32, kind="ExternalInput").ap()
    tf = nc.dram_tensor("tf", [512, 1], F32, kind="ExternalInput").ap()
    rout = nc.dram_tensor("rout", [512, T], BF16, kind="ExternalOutput").ap()
    HT = T // 2
    p = P(nc)
    with ExitStack() as es:
        kb = es.enter_context(nc.sbuf_tensor("kb", [128, HT], BF16))
        vb = es.enter_context(nc.sbuf_tensor("vb", [128, HT], BF16))
        sb = es.enter_context(nc.sbuf_tensor("sb", [128, HT], BF16))
        outb = es.enter_context(nc.sbuf_tensor("outb", [128, HT], BF16))
        P0 = es.enter_context(nc.sbuf_tensor("P0", [128, HT], F32))
        P1 = es.enter_context(nc.sbuf_tensor("P1", [128, HT], F32))
        P2 = es.enter_context(nc.sbuf_tensor("P2", [128, HT], F32))
        P3 = es.enter_context(nc.sbuf_tensor("P3", [128, HT], F32))
        P4 = es.enter_context(nc.sbuf_tensor("P4", [128, HT], F32))
        P5 = es.enter_context(nc.sbuf_tensor("P5", [128, HT], F32))
        ewt = es.enter_context(nc.sbuf_tensor("ewt", [128, HT], F32))
        cc = es.enter_context(nc.sbuf_tensor("cc", [128, 8], F32))
        for dg in range(4):
            r0 = dg * 128
            p.dma(lambda e, r0=r0: e.dma_start(out=cc[:, 0:1], in_=td[r0:r0 + 128, :]))
            p.dma(lambda e, r0=r0: e.dma_start(out=cc[:, 1:2], in_=tf[r0:r0 + 128, :]))
            p.s(lambda e: e.activation(cc[:, 2:3], cc[:, 0:1], AF.Exp))
            p.v(lambda e: e.tensor_scalar_mul(cc[:, 3:4], cc[:, 2:3], -1.0))
            p.s(lambda e: e.activation(cc[:, 4:5], cc[:, 3:4], AF.Exp))
            p.s(lambda e: e.activation(cc[:, 5:6], cc[:, 1:2], AF.Exp))
            p.v(lambda e: e.memset(cc[:, 6:7], 0.0))
            p.v(lambda e: e.memset(cc[:, 7:8], 0.0))
            for hh in range(2):
                h0 = hh * HT
                p.dma(lambda e, r0=r0, h0=h0: e.dma_start(out=kb[:, :], in_=kT[r0:r0 + 128, h0:h0 + HT]))
                p.dma(lambda e, r0=r0, h0=h0: e.dma_start(out=vb[:, :], in_=vT[r0:r0 + 128, h0:h0 + HT]))
                p.dma(lambda e, r0=r0, h0=h0: e.dma_start(out=sb[:, :], in_=sT[r0:r0 + 128, h0:h0 + HT]))
                p.v(lambda e: e.memset(ewt[:, :], 1.0))
                p.v(lambda e: e.tensor_scalar_mul(ewt[:, :], ewt[:, :], cc[:, 4:5]))
                p.s(lambda e: e.activation(P0[:, :], kb[:, :], AF.Exp))
                p.v(lambda e: e.tensor_copy(P1[:, :], vb[:, :]))
                p.v(lambda e: e.tensor_tensor(P2[:, :], P0[:, :], P1[:, :], op=AL.mult))
                p.v(lambda e: e.tensor_scalar_mul(P1[:, :], P2[:, :], cc[:, 5:6]))
                p.v(lambda e: e.tensor_scalar_mul(P5[:, :], P0[:, :], cc[:, 5:6]))
                p.v(lambda e: e.tensor_tensor_scan(P4[:, :], ewt[:, :], P2[:, :], cc[:, 6:7], op0=AL.mult, op1=AL.add))
                p.v(lambda e: e.tensor_tensor_scan(P2[:, :], ewt[:, :], P0[:, :], cc[:, 7:8], op0=AL.mult, op1=AL.add))
                p.v(lambda e: e.tensor_tensor(P3[:, 1:HT], P1[:, 1:HT], P4[:, 0:HT - 1], op=AL.add))
                p.v(lambda e: e.tensor_tensor(P3[:, 0:1], P1[:, 0:1], cc[:, 6:7], op=AL.add))
                p.v(lambda e: e.tensor_tensor(P0[:, 1:HT], P5[:, 1:HT], P2[:, 0:HT - 1], op=AL.add))
                p.v(lambda e: e.tensor_tensor(P0[:, 0:1], P5[:, 0:1], cc[:, 7:8], op=AL.add))
                p.v(lambda e: e.tensor_copy(cc[:, 6:7], P4[:, HT - 1:HT]))
                p.v(lambda e: e.tensor_copy(cc[:, 7:8], P2[:, HT - 1:HT]))
                p.v(lambda e: e.reciprocal(P5[:, :], P0[:, :]))
                p.v(lambda e: e.tensor_tensor(P4[:, :], P3[:, :], P5[:, :], op=AL.mult))
                p.v(lambda e: e.tensor_copy(P0[:, :], sb[:, :]))
                p.v(lambda e: e.tensor_tensor(outb[:, :], P4[:, :], P0[:, :], op=AL.mult))
                p.dma(lambda e, r0=r0, h0=h0: e.dma_start(out=rout[r0:r0 + 128, h0:h0 + HT], in_=outb[:, :]))
        p.build()
    return nc


def build_E():
    nc = bass.Bass("TRN2", target_bir_lowering=False)
    invs = nc.dram_tensor("invs", [64, CAP + 1], F32, kind="ExternalInput").ap()
    xall = nc.dram_tensor("xall", [TT, C], F32, kind="ExternalInput").ap()
    egw = nc.dram_tensor("egw", [8, C, 2 * IEX], F32, kind="ExternalInput").ap()
    edw = nc.dram_tensor("edw", [8, IEX, C], F32, kind="ExternalInput").ap()
    idb = nc.dram_tensor("idb", [128, 128], BF16, kind="ExternalInput").ap()
    idf = nc.dram_tensor("idf", [128, 128], F32, kind="ExternalInput").ap()
    fout = nc.dram_tensor("fout", [8 * CAP, C], F32, kind="ExternalOutput").ap()
    p = P(nc)
    bufT, guf, egwb = [], [], []
    ctx = []
    for g in range(8):
        h = nc.sbuf_tensor("bufT%d" % g, [128, CAP], BF16); ctx.append(h); bufT.append(h.__enter__())
        h = nc.sbuf_tensor("guf%d" % g, [128, CAP], F32); ctx.append(h); guf.append(h.__enter__())
        h = nc.sbuf_tensor("egwb%d" % g, [128, 2 * IEX], BF16); ctx.append(h); egwb.append(h.__enter__())
    hbf, edwb = [], []
    for q in range(4):
        h = nc.sbuf_tensor("hbf%d" % q, [128, CAP], BF16); ctx.append(h); hbf.append(h.__enter__())
        h = nc.sbuf_tensor("edwb%d" % q, [128, C], BF16); ctx.append(h); edwb.append(h.__enter__())
    with ExitStack() as es:
        mi = es.enter_context(nc.sbuf_tensor("mi", [8, CAP + 1], F32))
        mj = es.enter_context(nc.sbuf_tensor("mj", [8, CAP + 1], F32))
        miT = es.enter_context(nc.sbuf_tensor("miT", [128, 32], F32))
        miTu = es.enter_context(nc.sbuf_tensor("miTu", [128, 32], U32))
        xgf = es.enter_context(nc.sbuf_tensor("xgf", [128, C], F32))
        xgb = es.enter_context(nc.sbuf_tensor("xgb", [128, C], BF16))
        ids = es.enter_context(nc.sbuf_tensor("ids", [128, 128], BF16))
        idfs = es.enter_context(nc.sbuf_tensor("idfs", [128, 128], F32))
        wld = es.enter_context(nc.sbuf_tensor("wld", [128, 2 * IEX], F32))
        sg = es.enter_context(nc.sbuf_tensor("sg", [128, CAP], F32))
        oev = es.enter_context(nc.sbuf_tensor("oev", [128, 512], F32))
        pst = es.enter_context(nc.psum_tensor("pst", [128, 128], BF16))
        pstf = es.enter_context(nc.psum_tensor("pstf", [128, 8], F32))
        psg = es.enter_context(nc.psum_tensor("psg", [128, CAP], F32))
        psd = es.enter_context(nc.psum_tensor("psd", [128, 512], F32))
        p.dma(lambda e: e.dma_start(out=ids[:, :], in_=idb[:, :]))
        p.dma(lambda e: e.dma_start(out=idfs[:, :], in_=idf[:, :]))
        p.dma(lambda e: e.dma_start(out=mi[:, :], in_=invs[0:8, :]))
        for j in range(1, 8):
            p.dma(lambda e, j=j: e.dma_start(out=mj[:, :], in_=invs[8 * j:8 * j + 8, :]))
            p.v(lambda e: e.tensor_tensor(mi[:, :], mi[:, :], mj[:, :], op=AL.add))
        p.v(lambda e: e.tensor_scalar(mi[:, :], mi[:, :], -1.0, 0.0, op0=AL.add, op1=AL.max))
        for ch in range(4):
            p.t(lambda e, ch=ch: e.transpose(pstf[:, 0:8], mi[0:8, ch * 128:(ch + 1) * 128], idfs[0:8, 0:8]))
            p.v(lambda e, ch=ch: e.tensor_copy(miT[:, ch * 8:(ch + 1) * 8], pstf[:, 0:8]))
        p.v(lambda e: e.tensor_copy(miTu[:, :], miT[:, :]))
        for el in range(8):
            for ch in range(4):
                p.gdma(lambda e, el=el, ch=ch: e.indirect_dma_start(xgf[:, :], None, xall[:, :], IndirectOffsetOnAxis(ap=miTu[:, ch * 8 + el:ch * 8 + el + 1], axis=0)))
                p.v(lambda e: e.tensor_copy(xgb[:, :], xgf[:, :]))
                for cb in range(8):
                    p.t(lambda e, cb=cb: e.transpose(pst[:, :], xgb[:, cb * 128:(cb + 1) * 128], ids[:, :]))
                    p.v(lambda e, cb=cb, ch=ch: e.tensor_copy(bufT[cb][:, ch * 128:(ch + 1) * 128], pst[:, :]))
            for g in range(8):
                p.dma(lambda e, el=el, g=g: e.dma_start(out=wld[:, :], in_=egw[el, g * 128:(g + 1) * 128, :]))
                p.v(lambda e, g=g: e.tensor_copy(egwb[g][:, :], wld[:, :]))
            for mg in range(8):
                for g in range(8):
                    p.t(lambda e, mg=mg, g=g: e.matmul(psg[:, :], egwb[g][:, mg * 128:(mg + 1) * 128], bufT[g][:, :], start=(g == 0), stop=(g == 7)))
                p.v(lambda e, mg=mg: e.tensor_copy(guf[mg][:, :], psg[:, :]))
            for q in range(4):
                p.s(lambda e, q=q: e.activation(sg[:, :], guf[q][:, :], AF.Silu))
                p.v(lambda e, q=q: e.tensor_tensor(hbf[q][:, :], sg[:, :], guf[q + 4][:, :], op=AL.mult))
            for q in range(4):
                p.dma(lambda e, el=el, q=q: e.dma_start(out=wld[:, 0:C], in_=edw[el, q * 128:(q + 1) * 128, :]))
                p.v(lambda e, q=q: e.tensor_copy(edwb[q][:, :], wld[:, 0:C]))
            for sc in range(4):
                for n2 in range(2):
                    for q in range(4):
                        p.t(lambda e, sc=sc, n2=n2, q=q: e.matmul(psd[:, :], hbf[q][:, sc * 128:(sc + 1) * 128], edwb[q][:, n2 * 512:(n2 + 1) * 512], start=(q == 0), stop=(q == 3)))
                    p.v(lambda e: e.tensor_copy(oev[:, :], psd[:, :]))
                    p.dma(lambda e, el=el, sc=sc, n2=n2: e.dma_start(out=fout[el * CAP + sc * 128:el * CAP + (sc + 1) * 128, n2 * 512:(n2 + 1) * 512], in_=oev[:, :]))
        p.build()
    for h in reversed(ctx):
        h.__exit__(None, None, None)
    return nc


def build_F():
    nc = bass.Bass("TRN2", target_bir_lowering=False)
    xown = nc.dram_tensor("xown", [NTOK, C], F32, kind="ExternalInput").ap()
    xnT = nc.dram_tensor("xnT", [C, NTOK], F32, kind="ExternalInput").ap()
    rwk = nc.dram_tensor("rwk", [C, NTOK], BF16, kind="ExternalInput").ap()
    woT = nc.dram_tensor("woT", [C, C], F32, kind="ExternalInput").ap()
    sgu = nc.dram_tensor("sgu", [C, 2 * ISH], F32, kind="ExternalInput").ap()
    sdw = nc.dram_tensor("sdw", [ISH, C], F32, kind="ExternalInput").ap()
    sgw = nc.dram_tensor("sgw", [C, 1], F32, kind="ExternalInput").ap()
    fall = nc.dram_tensor("fall", [TT, C], F32, kind="ExternalInput").ap()
    eidx = nc.dram_tensor("eidx", [NTOK, TOPK], F32, kind="ExternalInput").ap()
    slo = nc.dram_tensor("slo", [NTOK, TOPK], F32, kind="ExternalInput").ap()
    w8 = nc.dram_tensor("w8", [NTOK, TOPK], F32, kind="ExternalInput").ap()
    idb = nc.dram_tensor("idb", [128, 128], BF16, kind="ExternalInput").ap()
    yout = nc.dram_tensor("yout", [NTOK, C], F32, kind="ExternalOutput").ap()
    p = P(nc)
    ctx = []
    sgub, sdwb, wob = [], [], []
    for g in range(8):
        h = nc.sbuf_tensor("sgub%d" % g, [128, 2 * ISH], BF16); ctx.append(h); sgub.append(h.__enter__())
        h = nc.sbuf_tensor("wob%d" % g, [128, C], BF16); ctx.append(h); wob.append(h.__enter__())
    for g in range(16):
        h = nc.sbuf_tensor("sdwb%d" % g, [128, C], BF16); ctx.append(h); sdwb.append(h.__enter__())
    with ExitStack() as es:
        wld = es.enter_context(nc.sbuf_tensor("wld", [128, 2 * ISH], F32))
        sgwc = es.enter_context(nc.sbuf_tensor("sgwc", [128, 8], BF16))
        ids = es.enter_context(nc.sbuf_tensor("ids", [128, 128], BF16))
        xtc = es.enter_context(nc.sbuf_tensor("xtc", [128, 8 * 128], BF16))
        rtc = es.enter_context(nc.sbuf_tensor("rtc", [128, 8 * 128], BF16))
        attn = es.enter_context(nc.sbuf_tensor("attn", [128, C], F32))
        gu = es.enter_context(nc.sbuf_tensor("gu", [128, 2 * ISH], F32))
        sil = es.enter_context(nc.sbuf_tensor("sil", [128, ISH], F32))
        hb = es.enter_context(nc.sbuf_tensor("hb", [128, ISH], BF16))
        hT = es.enter_context(nc.sbuf_tensor("hT", [128, ISH], BF16))
        shr = es.enter_context(nc.sbuf_tensor("shr", [128, C], F32))
        gcol = es.enter_context(nc.sbuf_tensor("gcol", [128, 2], F32))
        ymoe = es.enter_context(nc.sbuf_tensor("ymoe", [128, C], F32))
        xg = es.enter_context(nc.sbuf_tensor("xg", [128, C], F32))
        e8 = es.enter_context(nc.sbuf_tensor("e8", [128, TOPK], F32))
        s8 = es.enter_context(nc.sbuf_tensor("s8", [128, TOPK], F32))
        v8 = es.enter_context(nc.sbuf_tensor("v8", [128, TOPK], F32))
        sidx = es.enter_context(nc.sbuf_tensor("sidx", [128, 1], F32))
        sidxu = es.enter_context(nc.sbuf_tensor("sidxu", [128, 1], U32))
        xt = es.enter_context(nc.sbuf_tensor("xt", [128, C], F32))
        ot = es.enter_context(nc.sbuf_tensor("ot", [128, C], F32))
        psA = es.enter_context(nc.psum_tensor("psA", [128, 512], F32))
        psG = es.enter_context(nc.psum_tensor("psG", [128, 512], F32))
        psD = es.enter_context(nc.psum_tensor("psD", [128, 512], F32))
        psg1 = es.enter_context(nc.psum_tensor("psg1", [128, 1], F32))
        pst = es.enter_context(nc.psum_tensor("pst", [128, 128], BF16))
        p.dma(lambda e: e.dma_start(out=ids[:, :], in_=idb[:, :]))
        for g in range(8):
            p.dma(lambda e, g=g: e.dma_start(out=wld[:, :], in_=sgu[g * 128:(g + 1) * 128, :]))
            p.v(lambda e, g=g: e.tensor_copy(sgub[g][:, :], wld[:, :]))
            p.dma(lambda e, g=g: e.dma_start(out=wld[:, 0:C], in_=woT[g * 128:(g + 1) * 128, :]))
            p.v(lambda e, g=g: e.tensor_copy(wob[g][:, :], wld[:, 0:C]))
            p.dma(lambda e, g=g: e.dma_start(out=wld[:, 0:1], in_=sgw[g * 128:(g + 1) * 128, :]))
            p.v(lambda e, g=g: e.tensor_copy(sgwc[:, g:g + 1], wld[:, 0:1]))
        for g in range(16):
            p.dma(lambda e, g=g: e.dma_start(out=wld[:, 0:C], in_=sdw[g * 128:(g + 1) * 128, :]))
            p.v(lambda e, g=g: e.tensor_copy(sdwb[g][:, :], wld[:, 0:C]))
        for tc in range(NTOK // 128):
            t0 = tc * 128
            for g in range(8):
                p.dma(lambda e, g=g, t0=t0: e.dma_start(out=wld[:, 0:128], in_=xnT[g * 128:(g + 1) * 128, t0:t0 + 128]))
                p.v(lambda e, g=g: e.tensor_copy(xtc[:, g * 128:(g + 1) * 128], wld[:, 0:128]))
                p.dma(lambda e, g=g, t0=t0: e.dma_start(out=rtc[:, g * 128:(g + 1) * 128], in_=rwk[g * 128:(g + 1) * 128, t0:t0 + 128]))
            for n2 in range(2):
                for g in range(8):
                    p.t(lambda e, n2=n2, g=g: e.matmul(psA[:, :], rtc[:, g * 128:(g + 1) * 128], wob[g][:, n2 * 512:(n2 + 1) * 512], start=(g == 0), stop=(g == 7)))
                p.v(lambda e, n2=n2: e.tensor_copy(attn[:, n2 * 512:(n2 + 1) * 512], psA[:, :]))
            for n8 in range(8):
                for g in range(8):
                    p.t(lambda e, n8=n8, g=g: e.matmul(psG[:, :], xtc[:, g * 128:(g + 1) * 128], sgub[g][:, n8 * 512:(n8 + 1) * 512], start=(g == 0), stop=(g == 7)))
                p.v(lambda e, n8=n8: e.tensor_copy(gu[:, n8 * 512:(n8 + 1) * 512], psG[:, :]))
            p.s(lambda e: e.activation(sil[:, :], gu[:, 0:ISH], AF.Silu))
            p.v(lambda e: e.tensor_tensor(sil[:, :], sil[:, :], gu[:, ISH:2 * ISH], op=AL.mult))
            p.v(lambda e: e.tensor_copy(hb[:, :], sil[:, :]))
            for q in range(16):
                p.t(lambda e, q=q: e.transpose(pst[:, :], hb[:, q * 128:(q + 1) * 128], ids[:, :]))
                p.v(lambda e, q=q: e.tensor_copy(hT[:, q * 128:(q + 1) * 128], pst[:, :]))
            for n2 in range(2):
                for q in range(16):
                    p.t(lambda e, n2=n2, q=q: e.matmul(psD[:, :], hT[:, q * 128:(q + 1) * 128], sdwb[q][:, n2 * 512:(n2 + 1) * 512], start=(q == 0), stop=(q == 15)))
                p.v(lambda e, n2=n2: e.tensor_copy(shr[:, n2 * 512:(n2 + 1) * 512], psD[:, :]))
            for g in range(8):
                p.t(lambda e, g=g: e.matmul(psg1[:, :], xtc[:, g * 128:(g + 1) * 128], sgwc[:, g:g + 1], start=(g == 0), stop=(g == 7)))
            p.s(lambda e: e.activation(gcol[:, 0:1], psg1[:, :], AF.Sigmoid))
            p.v(lambda e: e.tensor_scalar_mul(shr[:, :], shr[:, :], gcol[:, 0:1]))
            p.v(lambda e: e.memset(ymoe[:, :], 0.0))
            p.dma(lambda e, t0=t0: e.dma_start(out=e8[:, :], in_=eidx[t0:t0 + 128, :]))
            p.dma(lambda e, t0=t0: e.dma_start(out=s8[:, :], in_=slo[t0:t0 + 128, :]))
            p.dma(lambda e, t0=t0: e.dma_start(out=v8[:, :], in_=w8[t0:t0 + 128, :]))
            for r in range(TOPK):
                p.v(lambda e, r=r: e.tensor_scalar(sidx[:, :], e8[:, r:r + 1], float(CAP), s8[:, r:r + 1], op0=AL.mult, op1=AL.add))
                p.v(lambda e: e.tensor_scalar(sidx[:, :], sidx[:, :], float(TT - 1), None, op0=AL.min))
                p.v(lambda e: e.tensor_copy(sidxu[:, :], sidx[:, :]))
                p.gdma(lambda e: e.indirect_dma_start(xg[:, :], None, fall[:, :], IndirectOffsetOnAxis(ap=sidxu[:, :], axis=0)))
                p.v(lambda e, r=r: e.tensor_scalar_mul(xg[:, :], xg[:, :], v8[:, r:r + 1]))
                p.v(lambda e: e.tensor_tensor(ymoe[:, :], ymoe[:, :], xg[:, :], op=AL.add))
            p.dma(lambda e, t0=t0: e.dma_start(out=xt[:, :], in_=xown[t0:t0 + 128, :]))
            p.v(lambda e: e.tensor_tensor(ot[:, :], xt[:, :], attn[:, :], op=AL.add))
            p.v(lambda e: e.tensor_tensor(ot[:, :], ot[:, :], shr[:, :], op=AL.add))
            p.v(lambda e: e.tensor_tensor(ot[:, :], ot[:, :], ymoe[:, :], op=AL.add))
            p.dma(lambda e, t0=t0: e.dma_start(out=yout[t0:t0 + 128, :], in_=ot[:, :]))
        p.build()
    for h in reversed(ctx):
        h.__exit__(None, None, None)
    return nc


def kernel(L0_attn_x, L0_attn_time_decay, L0_attn_time_first, L0_attn_time_mix_k,
           L0_attn_time_mix_v, L0_attn_time_mix_r, L0_attn_Wk, L0_attn_Wv, L0_attn_Wr,
           L0_attn_Wo, L0_ffn_gate_w, L0_ffn_shared_gate_w, L0_ffn_shared_gate_up_w,
           L0_ffn_shared_down_w, L0_ffn_expert_gate_up_w, L0_ffn_expert_down_w):
    f32 = np.float32
    x = np.ascontiguousarray(np.asarray(L0_attn_x, dtype=f32)).reshape(TT, C)
    g0 = [(i // 2) * T + (i % 2) * NTOK for i in range(8)]

    ncA = build_A()
    inA = [{"xin": np.ascontiguousarray(x[g0[i]:g0[i] + NTOK])} for i in range(8)]
    (xns,) = _run(ncA, inA, ["xn"])
    xn_all = np.ascontiguousarray(np.concatenate(xns, axis=0))

    io64 = np.tile(np.arange(E, dtype=f32), (128, 1))
    gw = np.ascontiguousarray(np.asarray(L0_ffn_gate_w, dtype=f32))
    xnT_own = [np.ascontiguousarray(xns[i].T) for i in range(8)]

    ncB = build_B()
    inB = [{"xnT": xnT_own[i], "gw": gw, "io64": io64} for i in range(8)]
    sels, eidxs, wraws = _run(ncB, inB, ["sel", "eidx", "wraw"])
    sel_all = np.ascontiguousarray(np.concatenate(sels, axis=0))

    ustr = np.triu(np.ones((128, 128), dtype=f32), 1)
    onec = np.ones((128, 1), dtype=f32)
    one1 = np.ones((1, 128), dtype=f32)
    id128 = np.eye(128, dtype=f32)
    ncC = build_C2()
    inC = []
    for i in range(8):
        tg0 = g0[i] // 128
        pm = np.zeros((256, 32), dtype=f32)
        for j in range(32):
            pm[:tg0 + j, j] = 1.0
        tok0 = (g0[i] + np.arange(128, dtype=f32)).reshape(128, 1)
        inC.append({"sall": sel_all, "sown": sels[i], "eidx": eidxs[i], "wraw": wraws[i],
                    "ustr": ustr, "onec": onec, "one1": one1, "pmask": pm, "io64": io64,
                    "tok0": tok0, "id128": id128})
    invs, w8s, slos = _run(ncC, inC, ["inv", "w8", "slo"])

    ncD1 = build_D1()
    WkT = np.ascontiguousarray(np.asarray(L0_attn_Wk, dtype=f32).T)
    WvT = np.ascontiguousarray(np.asarray(L0_attn_Wv, dtype=f32).T)
    WrT = np.ascontiguousarray(np.asarray(L0_attn_Wr, dtype=f32).T)
    tmk = np.asarray(L0_attn_time_mix_k, dtype=f32).reshape(C, 1)
    tmv = np.asarray(L0_attn_time_mix_v, dtype=f32).reshape(C, 1)
    tmr = np.asarray(L0_attn_time_mix_r, dtype=f32).reshape(C, 1)
    inD1 = []
    for i in range(8):
        b, h = i // 2, i % 2
        seg = xn_all[b * T + h * NTOK: b * T + (h + 1) * NTOK]
        if h == 0:
            prev = np.zeros((1, C), dtype=f32)
        else:
            prev = xn_all[b * T + h * NTOK - 1: b * T + h * NTOK]
        xov = np.ascontiguousarray(np.concatenate([prev, seg], axis=0).T)
        inD1.append({"xov": xov, "wkt": WkT, "wvt": WvT, "wrt": WrT,
                     "tmk": tmk, "tmv": tmv, "tmr": tmr})
    kouts, vouts, srouts = _run(ncD1, inD1, ["kout", "vout", "srout"])

    ncD2 = build_D2()
    td = np.asarray(L0_attn_time_decay, dtype=f32).reshape(C, 1)
    tf = np.asarray(L0_attn_time_first, dtype=f32).reshape(C, 1)
    inD2 = []
    for i in range(8):
        b, hc = i // 2, i % 2
        kTb = np.concatenate([kouts[2 * b], kouts[2 * b + 1]], axis=1)
        vTb = np.concatenate([vouts[2 * b], vouts[2 * b + 1]], axis=1)
        sTb = np.concatenate([srouts[2 * b], srouts[2 * b + 1]], axis=1)
        r0 = hc * 512
        inD2.append({"kT": np.ascontiguousarray(kTb[r0:r0 + 512]),
                     "vT": np.ascontiguousarray(vTb[r0:r0 + 512]),
                     "sT": np.ascontiguousarray(sTb[r0:r0 + 512]),
                     "td": np.ascontiguousarray(td[r0:r0 + 512]),
                     "tf": np.ascontiguousarray(tf[r0:r0 + 512])})
    (routs,) = _run(ncD2, inD2, ["rout"])

    ncE = build_E()
    inv_all = np.ascontiguousarray(np.concatenate(invs, axis=0))
    egw = np.asarray(L0_ffn_expert_gate_up_w, dtype=f32)
    edw = np.asarray(L0_ffn_expert_down_w, dtype=f32)
    idb = np.eye(128).astype(BF)
    inE = []
    for i in range(8):
        iv = np.zeros((64, CAP + 1), dtype=f32)
        for j in range(8):
            iv[8 * j:8 * j + 8] = invs[j][8 * i:8 * i + 8]
        inE.append({"invs": np.ascontiguousarray(iv), "xall": xn_all,
                    "egw": np.ascontiguousarray(egw[8 * i:8 * i + 8]),
                    "edw": np.ascontiguousarray(edw[8 * i:8 * i + 8]), "idb": idb,
                    "idf": id128})
    (fouts,) = _run(ncE, inE, ["fout"])
    fall = np.ascontiguousarray(np.concatenate(fouts, axis=0))

    ncF = build_F()
    WoT = np.ascontiguousarray(np.asarray(L0_attn_Wo, dtype=f32).T)
    sgu = np.ascontiguousarray(np.asarray(L0_ffn_shared_gate_up_w, dtype=f32))
    sdw = np.ascontiguousarray(np.asarray(L0_ffn_shared_down_w, dtype=f32))
    sgw = np.ascontiguousarray(np.asarray(L0_ffn_shared_gate_w, dtype=f32))
    inF = []
    for i in range(8):
        b, h = i // 2, i % 2
        rb = np.concatenate([routs[2 * b], routs[2 * b + 1]], axis=0)
        rwkT = np.ascontiguousarray(rb[:, h * NTOK:(h + 1) * NTOK])
        inF.append({"xown": np.ascontiguousarray(x[g0[i]:g0[i] + NTOK]),
                    "xnT": xnT_own[i], "rwk": rwkT, "woT": WoT, "sgu": sgu,
                    "sdw": sdw, "sgw": sgw, "fall": fall, "eidx": eidxs[i],
                    "slo": slos[i], "w8": w8s[i], "idb": idb})
    (youts,) = _run(ncF, inF, ["yout"])
    y = np.concatenate(youts, axis=0).reshape(B, T, C).astype(np.float32)
    return y
